# version 05
# baseline (speedup 1.0000x reference)
import numpy as np
import ml_dtypes
import concourse.bass as bass
import concourse.mybir as mybir
from concourse.bass_utils import run_bass_kernel_spmd

F32 = mybir.dt.float32
BF16 = mybir.dt.bfloat16
I32 = mybir.dt.int32
AF = mybir.ActivationFunctionType
OP = mybir.AluOpType
AX = mybir.AxisListType

NB = 2
S = 2048
D = 1024
H = 16
DH = 64
FF = 4096
NE = 4
ST = S // 128
DC = D // 128
EPS = 1e-5
SLAB = 2048
ENGINES = ("pe", "act", "dve", "pool", "sp")


def _ap_region(ap):
    t = ap.tensor
    name = t.name
    dims = [list(d) for d in ap.ap]
    off = ap.offset
    kindname = type(t).__name__
    if "DRam" in kindname:
        lo = off
        hi = off
        for stp, ct in dims:
            if ct > 1:
                if stp >= 0:
                    hi += stp * (ct - 1)
                else:
                    lo += stp * (ct - 1)
        return (name, 0, 128, lo, hi + 1)
    pstep, pcnt = dims[0]
    if pstep != 0:
        p0 = off // pstep
        fo = off - p0 * pstep
    else:
        p0, fo = 0, off
    lo = fo
    hi = fo
    for stp, ct in dims[1:]:
        if ct > 1:
            if stp >= 0:
                hi += stp * (ct - 1)
            else:
                lo += stp * (ct - 1)
    base = getattr(t, "base_partition", 0) or 0
    if "PSum" in kindname:
        lo = (lo // 512) * 512
        hi = ((hi // 512) + 1) * 512 - 1
        return (name, 0, 128, lo, hi + 1)
    return (name, base + p0, base + p0 + pcnt, lo, hi + 1)


def _overlap(a, b):
    return not (a[2] <= b[1] or b[2] <= a[1] or a[4] <= b[3] or b[4] <= a[3])


def _covers(a, b):
    return a[1] <= b[1] and a[2] >= b[2] and a[3] <= b[3] and a[4] >= b[4]


class _Op:
    __slots__ = ("stream", "space", "emit", "is_dma", "sig", "deps", "rdeps",
                 "snap", "seq", "waits", "dbg", "idx")

    def __init__(self, stream, space, emit, is_dma, dbg):
        self.stream = stream
        self.space = space
        self.emit = emit
        self.is_dma = is_dma
        self.sig = False
        self.deps = []
        self.rdeps = []
        self.snap = None
        self.seq = 0
        self.waits = []
        self.dbg = dbg
        self.idx = 0


class Sched:
    def __init__(self):
        self.ops = []
        self.hist = {}

    def _slabs(self, reg):
        return range(reg[3] // SLAB, (reg[4] - 1) // SLAB + 1)

    def op(self, stream, emit, reads=(), writes=(), queue=None, dbg=""):
        space = queue if queue else stream
        o = _Op(stream, space, emit, queue is not None, dbg)
        deps = set()
        rregs = [_ap_region(a) for a in reads if a is not None and hasattr(a, "tensor")]
        wregs = [_ap_region(a) for a in writes if a is not None and hasattr(a, "tensor")]
        for r in rregs:
            for sl in self._slabs(r):
                h = self.hist.get((r[0], sl))
                if not h:
                    continue
                for wr, wop in reversed(h["w"]):
                    if _overlap(wr, r):
                        deps.add(wop)
                        if _covers(wr, r):
                            break
        for w in wregs:
            for sl in self._slabs(w):
                h = self.hist.get((w[0], sl))
                if not h:
                    continue
                for wr, wop in reversed(h["w"]):
                    if _overlap(wr, w):
                        deps.add(wop)
                        if _covers(wr, w):
                            break
                for rr, rop in h["r"]:
                    if _overlap(rr, w):
                        deps.add(rop)
        for r in rregs:
            for sl in self._slabs(r):
                self.hist.setdefault((r[0], sl), {"w": [], "r": []})["r"].append((r, o))
        for w in wregs:
            for sl in self._slabs(w):
                h = self.hist.setdefault((w[0], sl), {"w": [], "r": []})
                h["w"] = [e for e in h["w"] if not _covers(w, e[0])]
                h["w"].append((w, o))
                h["r"] = [e for e in h["r"] if not _covers(w, e[0])]
        deps.discard(o)
        o.deps = list(deps)
        o.idx = len(self.ops)
        self.ops.append(o)
        return o

    def finish(self, nc, block):
        for o in self.ops:
            best = {}
            for d in o.deps:
                if (not d.is_dma) and d.stream == o.stream == "pe":
                    continue
                if d.idx > best.get(d.space, (-1, None))[0]:
                    best[d.space] = (d.idx, d)
            o.rdeps = [d for _, d in best.values()]
            for d in o.rdeps:
                d.sig = True
        for o in self.ops:
            if o.is_dma:
                o.sig = True
        counts = {}
        for o in self.ops:
            if o.sig:
                step = 16 if o.is_dma else 1
                counts[o.space] = counts.get(o.space, 0) + step
                o.seq = counts[o.space]
        clocks = {e: {} for e in ENGINES}
        for o in self.ops:
            ck = clocks[o.stream]
            pend = {}
            psnap = {}
            for d in o.rdeps:
                if d.seq > pend.get(d.space, 0):
                    pend[d.space] = d.seq
                    psnap[d.space] = d.snap
            for k in sorted(pend):
                v = pend[k]
                if ck.get(k, 0) >= v:
                    continue
                o.waits.append((k, v))
                ck[k] = v
                sn = psnap[k]
                if sn:
                    for k2, v2 in sn.items():
                        if ck.get(k2, 0) < v2:
                            ck[k2] = v2
            if o.sig:
                snap = dict(ck)
                snap[o.space] = max(snap.get(o.space, 0), o.seq)
                o.snap = snap
        all_spaces = set(counts.keys())
        for o in self.ops:
            all_spaces.add(o.space)
        sems = {}
        for sp_name in sorted(all_spaces):
            sems[sp_name] = nc.semaphore(f"sem_{sp_name}").__enter__()
        per_stream = {e: [] for e in ENGINES}
        for o in self.ops:
            per_stream[o.stream].append(o)

        def emit_stream(eng, stream):
            for o in per_stream[stream]:
                for s, v in o.waits:
                    eng.wait_ge(sems[s], v)
                instr = o.emit(eng)
                if o.sig:
                    instr.then_inc(sems[o.space], 16 if o.is_dma else 1)

        @block.tensor
        def _(eng):
            emit_stream(eng, "pe")

        @block.scalar
        def _(eng):
            emit_stream(eng, "act")

        @block.vector
        def _(eng):
            emit_stream(eng, "dve")

        @block.gpsimd
        def _(eng):
            emit_stream(eng, "pool")

        @block.sync
        def _(eng):
            emit_stream(eng, "sp")


ABN = 73984
XNT = 0
ATP = 0
X1M = 0
W2B = 8192
QA = 16384
KA = 32768
W1B = 16384
VAT = 49152
WB6 = 49152
HSB = 49152
UST = 65792
WB0 = 65792
PT0 = 65792
X1SL = 65536
AFN = 6144
AF_T0 = 0
AF_T1 = 1024
AF_BC0 = 2048
AF_BC1 = 3072
AF_BC2 = 4096
AF_RB = 5120


def build_core(debug=False):
    import contextlib
    nc = bass.Bass("TRN2", target_bir_lowering=False)
    ctx = contextlib.ExitStack()
    S_ = Sched()

    dt = {}

    def din(name, shape, dtype=F32):
        dt[name] = nc.dram_tensor(name, shape, dtype, kind="ExternalInput").ap()
        return dt[name]

    x_in = din("xsh", [NB, S, D])
    for n, shp in [("aWq", [D, D]), ("aWk", [D, D]), ("aWv", [D, D]), ("aWo", [D, D]),
                   ("abo", [D]), ("fWq", [D, D]), ("fbq", [D]), ("fWk", [D, D]),
                   ("fbk", [D]), ("fWv", [D, D]), ("fbv", [D]), ("fWo", [D, D]),
                   ("fbo", [D]), ("frw", [D, 1]), ("frb", [1]),
                   ("fW1", [NE, D, FF]), ("fb1", [NE, FF]),
                   ("fW2", [NE, FF, D]), ("fb2", [NE, D]),
                   ("g1", [D]), ("be1", [D]), ("g2", [D]), ("be2", [D])]:
        din(n, shp)
    din("ident", [128, 128])
    din("onesf", [128, 128])
    din("onesbf", [128, 128], BF16)

    y_out = nc.dram_tensor("y", [NB, S, D], F32, kind="ExternalOutput").ap()

    xn_dr = nc.dram_tensor("xn_dr", [NB, S, D], F32, kind="Internal").ap()
    x1_dr = nc.dram_tensor("x1_dr", [NB, S, D], F32, kind="Internal").ap()
    x1t_dr = nc.dram_tensor("x1t_dr", [NB, D, S], BF16, kind="Internal").ap()
    u_dr = nc.dram_tensor("u_dr", [NB, H, 128, D], BF16, kind="Internal").ap()
    sp_dr = nc.dram_tensor("sp_dr", [NB, S, D], F32, kind="Internal").ap()
    ff_dr = nc.dram_tensor("ff_dr", [NB, S, D], F32, kind="Internal").ap()
    rcol_dr = nc.dram_tensor("rcol_dr", [NE, D], F32, kind="Internal").ap()
    rrow_dr = nc.dram_tensor("rrow_dr", [1, D], F32, kind="Internal").ap()

    AB = ctx.enter_context(nc.sbuf_tensor("AB", [128, ABN], BF16))
    AFA = ctx.enter_context(nc.sbuf_tensor("AFar", [128, AFN], F32))
    PS = ctx.enter_context(nc.psum_tensor("PS", [128, 4096], F32))

    def sbt(name, shape, dtype):
        return ctx.enter_context(nc.sbuf_tensor(name, shape, dtype))

    ident = sbt("identsb", [128, 128], F32)
    onesf = sbt("onesfsb", [128, 128], F32)
    onesb = sbt("onesbsb", [128, 128], BF16)
    bqc = sbt("bqc", [128, DC], F32)
    bkc = sbt("bkc", [128, DC], F32)
    b1c = sbt("b1c", [128, 32 * NE], F32)
    rlb1 = sbt("rlb1", [128, 32 * NE], BF16)
    row_buf = sbt("row_buf", [1, 2 * D], F32)
    cmat = sbt("cmat", [1, NE * D], BF16)
    mrow = sbt("mrow", [1, NE * S], BF16)
    small = sbt("small", [128, 80], F32)
    smalli = sbt("smalli", [128, 2], I32)
    en_sb = sbt("en_sb", [16, 16], F32)
    att_sb = sbt("att_sb", [16, 16], F32)
    attT = sbt("attT", [16, 16], BF16)
    x1tstg = sbt("x1tstg", [128, 1024], BF16)

    ID = ident[:, :]

    class BD:
        def mm(self, out, lhsT, rhs, start, stop, dbg=""):
            S_.op("pe", lambda e: e.matmul(out, lhsT, rhs, start=start, stop=stop),
                  reads=[lhsT, rhs], writes=[out], dbg=dbg)

        def tp(self, out, in_, dbg="", p=128):
            idv = ident[0:p, 0:p]
            S_.op("pe", lambda e: e.transpose(out, in_, idv),
                  reads=[in_, idv], writes=[out], dbg=dbg)

        def actf(self, out, in_, func, bias=None, scale=None, accum=None, dbg=""):
            kw = {}
            if bias is not None:
                kw["bias"] = bias
            if scale is not None:
                kw["scale"] = scale
            if accum is not None:
                kw["accum_out"] = accum
            S_.op("act", lambda e: e.activation(out, in_, func, **kw),
                  reads=[in_, bias if hasattr(bias, "tensor") else None],
                  writes=[out, accum], dbg=dbg)

        def tt(self, out, a, b2, op, dbg=""):
            S_.op("dve", lambda e: e.tensor_tensor(out, a, b2, op),
                  reads=[a, b2], writes=[out], dbg=dbg)

        def ts(self, out, in0, s1, s2, op0, op1=None, accum=None, dbg=""):
            kw = {"accum_out": accum} if accum is not None else {}
            if op1 is not None:
                S_.op("dve", lambda e: e.tensor_scalar(out, in0, s1, s2, op0, op1, **kw),
                      reads=[in0, s1 if hasattr(s1, "tensor") else None,
                             s2 if hasattr(s2, "tensor") else None],
                      writes=[out, accum], dbg=dbg)
            else:
                S_.op("dve", lambda e: e.tensor_scalar(out, in0, s1, s2, op0, **kw),
                      reads=[in0, s1 if hasattr(s1, "tensor") else None],
                      writes=[out, accum], dbg=dbg)

        def stt(self, out, in0, scalar, in1, op0, op1, dbg=""):
            S_.op("dve", lambda e: e.scalar_tensor_tensor(out, in0, scalar, in1, op0, op1),
                  reads=[in0, in1], writes=[out], dbg=dbg)

        def cp(self, out, in_, eng="dve", dbg=""):
            if eng == "act":
                S_.op(eng, lambda e: e.copy(out, in_), reads=[in_], writes=[out], dbg=dbg)
            else:
                S_.op(eng, lambda e: e.tensor_copy(out, in_), reads=[in_], writes=[out], dbg=dbg)

        def cpa(self, out_ap, in_ap, wr_hint, eng="dve", dbg=""):
            if eng == "act":
                S_.op(eng, lambda e: e.copy(out_ap, in_ap),
                      reads=[in_ap], writes=[wr_hint], dbg=dbg)
            else:
                S_.op(eng, lambda e: e.tensor_copy(out_ap, in_ap),
                      reads=[in_ap], writes=[wr_hint], dbg=dbg)

        def red(self, out, in_, op, dbg=""):
            S_.op("dve", lambda e: e.tensor_reduce(out, in_, AX.X, op),
                  reads=[in_], writes=[out], dbg=dbg)

        def rcp(self, out, in_, dbg=""):
            S_.op("dve", lambda e: e.reciprocal(out, in_), reads=[in_], writes=[out], dbg=dbg)

        def mems(self, ap, val, wr_hint=None, dbg=""):
            S_.op("dve", lambda e: e.memset(ap, val), reads=[],
                  writes=[wr_hint if wr_hint is not None else ap], dbg=dbg)

        def dma(self, queue, out, in_, accum=False, slow=False, dbg=""):
            eng = {"qsp": "sp", "qact": "act", "qpool": "pool"}[queue]
            kw = {}
            if accum:
                kw["accum_op"] = OP.add
            if slow:
                kw["allow_slow_non_contiguous"] = True
            S_.op(eng, lambda e: e.dma_start(out=out, in_=in_, **kw),
                  reads=[in_, out if accum else None], writes=[out],
                  queue=queue, dbg=dbg)

    bd = BD()

    def abv(off, n, p0=0, pn=128):
        return AB[p0:p0 + pn, off:off + n]

    def afv(off, n, p0=0, pn=128):
        return AFA[p0:p0 + pn, off:off + n]

    def load_wDD(buf_off, wname):
        bd.dma("qpool",
               bass.AP(AB, buf_off, [[ABN, 128], [1024, 8], [1, 1024]]),
               dt[wname].rearrange("(c p) j -> p c j", p=128), dbg=f"ld{wname}")

    def bcast_ap(dst_off, src_row_ap, psbank, n=D, dbg="bc"):
        for nb2 in range(n // 512):
            po = PS[:, psbank * 512:(psbank + 1) * 512]
            bd.mm(po, onesf[0:1, 0:128], src_row_ap[0:1, nb2 * 512:(nb2 + 1) * 512],
                  True, True, dbg=dbg + "mm")
            bd.cp(afv(dst_off + nb2 * 512, 512), po, eng="dve", dbg=dbg + "ev")

    def bcast_in(dst_off, dram_row_ap, psbank, dbg="bci"):
        bd.dma("qsp", row_buf[0:1, 0:D], dram_row_ap, dbg=dbg + "ld")
        bcast_ap(dst_off, row_buf, psbank, dbg=dbg)

    def lnorm_tok(xt, disc, sc=0):
        t = afv(xt, D)
        bd.red(small[:, sc:sc + 1], t, OP.add, dbg="ln_sum")
        bd.ts(small[:, sc + 1:sc + 2], small[:, sc:sc + 1], 1.0 / D, None, OP.mult, dbg="ln_mu")
        bd.ts(t, t, small[:, sc + 1:sc + 2], None, OP.subtract, dbg="ln_sub")
        bd.actf(afv(disc, D), t, AF.Square, accum=small[:, sc + 2:sc + 3], dbg="ln_sq")
        bd.actf(small[:, sc + 3:sc + 4], small[:, sc + 2:sc + 3], AF.Sqrt,
                bias=small[:, 41:42], scale=1.0 / D, dbg="ln_sd")
        bd.rcp(small[:, sc + 4:sc + 5], small[:, sc + 3:sc + 4], dbg="ln_rcp")
        bd.ts(t, t, small[:, sc + 4:sc + 5], None, OP.mult, dbg="ln_nrm")

    bd.dma("qsp", ident[:, :], dt["ident"])
    bd.dma("qsp", onesf[:, :], dt["onesf"])
    bd.dma("qsp", onesb[:, :], dt["onesbf"])
    bd.dma("qsp", bqc[:, :], dt["fbq"].rearrange("(c p) -> p c", p=128), slow=True)
    bd.dma("qsp", bkc[:, :], dt["fbk"].rearrange("(c p) -> p c", p=128), slow=True)
    bd.dma("qsp", b1c[:, :], dt["fb1"].rearrange("e (c p) -> p (e c)", p=128), slow=True)
    bd.actf(rlb1[:, :], b1c[:, :], AF.Relu, dbg="relub1")
    bd.mems(small[:, 41:42], float(EPS), dbg="epscol")
    bd.dma("qsp", row_buf[0:1, 0:1], dt["frb"].rearrange("(a o) -> a o", a=1))
    po = PS[:, 7 * 512:7 * 512 + 1]
    bd.mm(po, onesf[0:1, 0:128], row_buf[0:1, 0:1], True, True, dbg="frbmm")
    bd.cp(small[:, 40:41], po, eng="dve", dbg="frbev")

    for b in range(NB):
        for st in range(ST):
            xt = AF_T0 if st % 2 == 0 else AF_T1
            bd.dma("qsp", afv(xt, D), x_in[b, st * 128:(st + 1) * 128, :], dbg=f"ldx{st}")
            lnorm_tok(xt, AF_BC1 if st % 2 == 0 else AF_BC2, sc=8 * (st % 2))
            bd.dma("qsp", xn_dr[b, st * 128:(st + 1) * 128, :], afv(xt, D), dbg=f"stxn{st}")
            for g in range(2):
                bank = (st % 2) * 2 + g
                for c4 in range(4):
                    c = g * 4 + c4
                    bd.tp(PS[:, bank * 512 + c4 * 128: bank * 512 + (c4 + 1) * 128],
                          afv(xt + c * 128, 128), dbg=f"tpxn{st}_{c}")
                pob = PS[:, bank * 512:(bank + 1) * 512]
                bd.cpa(bass.AP(AB, XNT + g * 4 * 2048 + st * 128,
                               [[ABN, 128], [2048, 4], [1, 128]]),
                       pob, abv(XNT + g * 4 * 2048 + st * 128, 3 * 2048 + 128),
                       eng=("dve" if st % 2 else "act"), dbg=f"evxnT{st}_{g}")

        load_wDD(WB0, "aWq")
        for (dst, tag) in ((QA, "q"), (KA, "k")):
            for st in range(ST):
                for nb2 in range(2):
                    bank = (st * 2 + nb2) % 4
                    po = PS[:, bank * 512:(bank + 1) * 512]
                    for c in range(DC):
                        bd.mm(po, abv(XNT + c * 2048 + st * 128, 128),
                              abv(WB0 + c * 1024 + nb2 * 512, 512),
                              c == 0, c == DC - 1, dbg=f"{tag}proj")
                    bd.cp(abv(dst + st * 1024 + nb2 * 512, 512), po,
                          eng=("act" if (st + nb2) % 2 else "dve"), dbg=f"{tag}ev")
            if dst == QA:
                load_wDD(WB0, "aWk")
        en_ps = PS[0:16, 4 * 512:4 * 512 + 16]
        first = True
        for d in range(DH):
            for st in range(ST):
                qv = bass.AP(AB, QA + st * 1024 + d, [[ABN, 128], [64, 16]])
                kv = bass.AP(AB, KA + st * 1024 + d, [[ABN, 128], [64, 16]])
                bd.mm(en_ps, qv, kv, first, (d == DH - 1 and st == ST - 1), dbg="energy")
                first = False
        load_wDD(WB0, "aWv")
        for co in range(DC):
            for sb4 in range(4):
                bank = (co * 4 + sb4) % 4
                po = PS[:, bank * 512:(bank + 1) * 512]
                for c in range(DC):
                    bd.mm(po, abv(WB0 + c * 1024 + co * 128, 128),
                          abv(XNT + c * 2048 + sb4 * 512, 512),
                          c == 0, c == DC - 1, dbg="vaT")
                bd.cp(abv(VAT + co * 2048 + sb4 * 512, 512), po,
                      eng=("act" if (co + sb4) % 2 else "dve"), dbg="vaTev")
        bd.red(small[0:16, 5:6], en_ps, OP.max, dbg="en_max")
        bd.ts(small[0:16, 6:7], small[0:16, 5:6], -0.125, None, OP.mult, dbg="en_nm")
        bd.actf(en_sb[:, :], en_ps, AF.Exp, bias=small[0:16, 6:7], scale=0.125,
                accum=small[0:16, 7:8], dbg="en_exp")
        bd.rcp(small[0:16, 8:9], small[0:16, 7:8], dbg="en_rcp")
        bd.ts(att_sb[:, :], en_sb[:, :], small[0:16, 8:9], None, OP.mult, dbg="en_att")
        attT_ps = PS[0:16, 5 * 512:5 * 512 + 16]
        bd.tp(attT_ps, att_sb[:, :], dbg="attT", p=16)
        bd.cp(attT[:, :], attT_ps, eng="dve", dbg="attTev")

        for t in range(16):
            bd.dma("qpool", abv(QA + t * 1024, 1024, 0, 64),
                   dt["aWo"][64 * t:64 * t + 64, :], dbg="woA")
            bd.dma("qpool", abv(KA + t * 1024, 1024, 64, 64),
                   dt["aWo"][64 * t:64 * t + 64, :], dbg="woB")
        for bb in range(16):
            pb = 64 * (bb % 2)
            wo = QA if pb == 0 else KA
            for nb2 in range(2):
                po = PS[:, nb2 * 512:(nb2 + 1) * 512]
                for j in range(8):
                    for half in range(2):
                        t = 2 * j + half
                        lv = bass.AP(AB, pb * ABN + VAT + (bb // 2) * 2048 + t,
                                     [[ABN, 64], [16, 128]])
                        rv = abv(wo + t * 1024 + nb2 * 512, 512, pb, 64)
                        bd.mm(po, lv, rv, (j == 0 and half == 0), (j == 7 and half == 1),
                              dbg=f"u{bb}")
                bd.cp(abv(UST + (bb % 2) * 1024 + nb2 * 512, 512), po,
                      eng=("act" if nb2 else "dve"), dbg="uev")
            bd.dma("qact", u_dr[b, bb], abv(UST + (bb % 2) * 1024, 1024), dbg=f"ust{bb}")

        spst0 = AFA[0:16, AF_BC0:AF_BC0 + 2048]
        spst1 = AFA[0:16, AF_BC2:AF_BC2 + 2048]
        for pi in range(8):
            uoff = QA if pi % 2 == 0 else KA
            bd.dma("qpool", abv(uoff, 16384, 0, 16),
                   u_dr[b, :, 16 * pi:16 * (pi + 1), :].rearrange("h r d2 -> h (r d2)"),
                   dbg=f"ldU{pi}")
            for qq in range(8):
                bset = 0 if qq % 2 == 0 else 4
                for k4 in range(4):
                    bd.mm(PS[0:16, (bset + k4) * 512:(bset + k4) * 512 + 512],
                          attT[:, :], abv(uoff + qq * 2048 + k4 * 512, 512, 0, 16),
                          True, True, dbg="spmm")
                pov = PS[0:16, bset * 512:(bset + 4) * 512]
                dstv = spst0 if qq % 2 == 0 else spst1
                bd.cp(dstv, pov, eng=("act" if qq % 2 else "dve"), dbg="spev")
                bd.dma("qsp",
                       sp_dr[b].rearrange("(h r) d2 -> h (r d2)", h=16)[
                           :, pi * 16384 + qq * 2048: pi * 16384 + (qq + 1) * 2048],
                       dstv, dbg="spst")

        bcast_in(AF_BC0, dt["fbv"].rearrange("(a d2) -> a d2", a=1), 6, dbg="bv")
        load_wDD(WB0, "fWq")
        for (wdst, bcol, tag) in ((QA, bqc, "qb"), (KA, bkc, "kb")):
            for co in range(DC):
                for sb4 in range(4):
                    bank = (co * 4 + sb4) % 4
                    po = PS[:, bank * 512:(bank + 1) * 512]
                    for c in range(DC):
                        bd.mm(po, abv(WB0 + c * 1024 + co * 128, 128),
                              abv(XNT + c * 2048 + sb4 * 512, 512),
                              c == 0, c == DC - 1, dbg=tag)
                    bd.ts(abv(wdst + co * 2048 + sb4 * 512, 512), po,
                          bcol[:, co:co + 1], None, OP.add, dbg=tag + "ev")
            if wdst == QA:
                load_wDD(WB0, "fWk")
        load_wDD(WB0, "fWv")
        for st in range(ST):
            for nb2 in range(2):
                bank = 4 + (st * 2 + nb2) % 2
                po = PS[:, bank * 512:(bank + 1) * 512]
                for c in range(DC):
                    bd.mm(po, abv(XNT + c * 2048 + st * 128, 128),
                          abv(WB0 + c * 1024 + nb2 * 512, 512),
                          c == 0, c == DC - 1, dbg="vaug")
                S_.op("dve",
                      (lambda po_, st_, nb2_: (lambda e: e.tensor_tensor(
                          bass.AP(AB, VAT + st_ * 1040 + nb2_ * 8 * 65,
                                  [[ABN, 128], [65, 8], [1, 64]]),
                          po_,
                          bass.AP(AFA, AF_BC0 + nb2_ * 512,
                                  [[AFN, 128], [64, 8], [1, 64]]),
                          OP.add)))(po, st, nb2),
                      reads=[po, afv(AF_BC0 + nb2 * 512, 512)],
                      writes=[abv(VAT + st * 1040 + nb2 * 520, 524)], dbg="vaugev")
            S_.op("dve",
                  (lambda st_: (lambda e: e.memset(
                      bass.AP(AB, VAT + st_ * 1040 + 64, [[ABN, 128], [65, 16], [1, 1]]),
                      1.0)))(st),
                  reads=[], writes=[abv(VAT + st * 1040, 1040)], dbg="vones")

        for h in range(H):
            hb = 64 * (h % 2)
            hc = h // 2
            for kt in range(ST):
                ptb = PT0 + (kt % 4) * 2048
                for qp in range(2):
                    bp = qp * 2
                    for qh in range(2):
                        qb4 = qp * 2 + qh
                        bd.mm(PS[:, (bp + qh) * 512:(bp + qh + 1) * 512],
                              bass.AP(AB, hb * ABN + KA + hc * 2048 + kt * 128,
                                      [[ABN, 64], [1, 128]]),
                              abv(QA + hc * 2048 + qb4 * 512, 512, hb, 64),
                              True, True, dbg=f"sc{h}")
                    bd.actf(abv(ptb + qp * 1024, 1024),
                            PS[:, bp * 512:(bp + 2) * 512], AF.Exp, scale=0.125,
                            dbg=f"exp{h}")
                for qb4 in range(4):
                    bd.mm(PS[0:65, (4 + qb4) * 512:(4 + qb4) * 512 + 512],
                          abv(VAT + kt * 1040 + 65 * h, 65),
                          abv(ptb + qb4 * 512, 512),
                          kt == 0, kt == ST - 1, dbg=f"pv{h}")
            for qb4 in range(4):
                den = PS[64:65, (4 + qb4) * 512:(4 + qb4) * 512 + 512]
                rrow = AFA[0:1, AF_BC0 + qb4 * 512:AF_BC0 + (qb4 + 1) * 512]
                bd.rcp(rrow, den, dbg=f"den{h}")
                bc = PS[0:64, qb4 * 512:(qb4 + 1) * 512]
                bd.mm(bc, onesf[0:1, 0:64], rrow, True, True, dbg=f"bcd{h}")
                bd.cp(afv(AF_BC2 + (qb4 % 2) * 512, 512, 0, 64), bc, eng="act",
                      dbg=f"bcev{h}")
                S_.op("dve",
                      (lambda h_, hb_, hc_, qb4_: (lambda e: e.tensor_tensor(
                          bass.AP(AB, hb_ * ABN + ATP + hc_ * 2048 + qb4_ * 512,
                                  [[ABN, 64], [1, 512]]),
                          PS[0:64, (4 + qb4_) * 512:(4 + qb4_) * 512 + 512],
                          afv(AF_BC2 + (qb4_ % 2) * 512, 512, 0, 64),
                          OP.mult)))(h, hb, hc, qb4),
                      reads=[PS[0:64, (4 + qb4) * 512:(4 + qb4) * 512 + 512],
                             afv(AF_BC2 + (qb4 % 2) * 512, 512, 0, 64)],
                      writes=[abv(ATP + hc * 2048 + qb4 * 512, 512, hb, 64)],
                      dbg=f"fin{h}")

        load_wDD(WB6, "fWo")
        bcast_in(AF_BC0, dt["fbo"].rearrange("(a d2) -> a d2", a=1), 6, dbg="bo")
        bcast_in(AF_BC1, dt["g1"].rearrange("(a d2) -> a d2", a=1), 6, dbg="g1")
        bcast_in(AF_BC2, dt["be1"].rearrange("(a d2) -> a d2", a=1), 6, dbg="be1")
        bcast_in(AF_RB, dt["frw"].rearrange("d2 o -> o d2"), 6, dbg="rw")
        for st in range(ST):
            for nb2 in range(2):
                bank = (st * 2 + nb2) % 2
                po = PS[:, bank * 512:(bank + 1) * 512]
                for c in range(DC):
                    bd.mm(po, abv(ATP + c * 2048 + st * 128, 128),
                          abv(WB6 + c * 1024 + nb2 * 512, 512),
                          c == 0, c == DC - 1, dbg="attout")
                bd.tt(afv(AF_T0 + nb2 * 512, 512), po, afv(AF_BC0 + nb2 * 512, 512),
                      OP.add, dbg="attbo")
            bd.dma("qsp", afv(AF_T1, D), xn_dr[b, st * 128:(st + 1) * 128, :], dbg="ldxn6")
            bd.tt(afv(AF_T0, D), afv(AF_T0, D), afv(AF_T1, D), OP.add, dbg="res1")
            lnorm_tok(AF_T0, AF_T1)
            bd.tt(afv(AF_T0, D), afv(AF_T0, D), afv(AF_BC1, D), OP.mult, dbg="ln1g")
            bd.tt(afv(AF_T0, D), afv(AF_T0, D), afv(AF_BC2, D), OP.add, dbg="ln1b")
            bd.dma("qsp", x1_dr[b, st * 128:(st + 1) * 128, :], afv(AF_T0, D), dbg="stx1")
            bd.tt(afv(AF_T1, D), afv(AF_T0, D), afv(AF_RB, D), OP.mult, dbg="keymul")
            bd.red(small[:, 16:17], afv(AF_T1, D), OP.add, dbg="keysum")
            bd.tt(small[:, 16:17], small[:, 16:17], small[:, 40:41], OP.add, dbg="keyb")
            kc = small[:, 16:17]
            bd.cp(smalli[:, 0:1], kc, dbg="fli")
            bd.cp(small[:, 17:18], smalli[:, 0:1], dbg="flf")
            bd.tt(small[:, 18:19], small[:, 17:18], kc, OP.is_gt, dbg="flg")
            bd.tt(small[:, 19:20], small[:, 17:18], small[:, 18:19], OP.subtract, dbg="fl")
            bd.ts(small[:, 20:21], small[:, 19:20], 0.25, None, OP.mult, dbg="fq")
            bd.cp(smalli[:, 1:2], small[:, 20:21], dbg="fqi")
            bd.cp(small[:, 21:22], smalli[:, 1:2], dbg="fqf")
            bd.tt(small[:, 22:23], small[:, 21:22], small[:, 20:21], OP.is_gt, dbg="fqg")
            bd.tt(small[:, 23:24], small[:, 21:22], small[:, 22:23], OP.subtract, dbg="flq")
            bd.stt(small[:, 24:25], small[:, 23:24], -4.0, small[:, 19:20],
                   OP.mult, OP.add, dbg="mmod")
            for e in range(NE):
                bd.ts(small[:, 26 + e:27 + e], small[:, 24:25], float(e), None,
                      OP.is_equal, dbg=f"mc{e}")
                bd.tp(PS[0:1, 6 * 512 + e * 128:6 * 512 + (e + 1) * 128],
                      small[:, 26 + e:27 + e], dbg=f"mtp{e}")
            for e in range(NE):
                bd.cp(mrow[0:1, e * S + st * 128:e * S + (st + 1) * 128],
                      PS[0:1, 6 * 512 + e * 128:6 * 512 + (e + 1) * 128],
                      eng="dve", dbg=f"mrow{e}")
            for g in range(2):
                bank = 2 + g
                for c4 in range(4):
                    c = g * 4 + c4
                    bd.tp(PS[:, bank * 512 + c4 * 128:bank * 512 + (c4 + 1) * 128],
                          afv(AF_T0 + c * 128, 128), dbg=f"tpx1{st}_{c}")
                bd.cp(x1tstg[:, g * 512:(g + 1) * 512],
                      PS[:, bank * 512:(bank + 1) * 512],
                      eng=("dve" if st % 2 else "act"), dbg=f"evx1{st}_{g}")
            bd.dma("qact",
                   bass.AP(x1t_dr.tensor, b * D * S + st * 128,
                           [[S, 128], [128 * S, 8], [1, 128]]),
                   bass.AP(x1tstg, 0, [[1024, 128], [128, 8], [1, 128]]),
                   dbg=f"stx1t{st}")

        for e in range(NE):
            bd.dma("qpool",
                   bass.AP(AB, W1B, [[ABN, 128], [4096, 8], [1, 4096]]),
                   dt["fW1"][e].rearrange("(c p) f -> p c f", p=128), dbg=f"ldW1{e}")
            for sbk in range(4):
                xsl = X1SL + (sbk % 2) * 4096
                bd.dma("qpool",
                       bass.AP(AB, xsl, [[ABN, 128], [512, 8], [1, 512]]),
                       x1t_dr[b].rearrange("(c p) s2 -> p c s2", p=128)[
                           :, :, sbk * 512:(sbk + 1) * 512],
                       dbg=f"ldx1t{e}{sbk}")
                pom = PS[:, 6 * 512:7 * 512]
                bd.mm(pom, onesb[0:1, 0:128],
                      mrow[0:1, e * S + sbk * 512:e * S + (sbk + 1) * 512],
                      True, True, dbg=f"mb{e}{sbk}")
                x1m = X1M + ((e * 4 + sbk) % 2) * 4096
                for c in range(DC):
                    bd.tt(abv(x1m + c * 512, 512), abv(xsl + c * 512, 512),
                          pom, OP.mult, dbg=f"x1m{e}{sbk}")
                for fc in range(32):
                    bank = fc % 4
                    po = PS[:, bank * 512:(bank + 1) * 512]
                    for c in range(DC):
                        bd.mm(po, abv(W1B + c * 4096 + fc * 128, 128),
                              abv(x1m + c * 512, 512), c == 0, c == DC - 1,
                              dbg=f"h{e}{sbk}")
                    bd.actf(abv(HSB + fc * 512, 512), po, AF.Relu,
                            bias=b1c[:, e * 32 + fc:e * 32 + fc + 1],
                            dbg=f"hrl{e}{sbk}")
                for dc2 in range(DC):
                    w2o = W2B + (dc2 % 2) * 4096
                    for hf in range(2):
                        bd.dma("qpool",
                               bass.AP(AB, w2o + hf * 2048, [[ABN, 128], [128, 16], [1, 128]]),
                               dt["fW2"][e, hf * 2048:(hf + 1) * 2048,
                                         dc2 * 128:(dc2 + 1) * 128].rearrange(
                                   "(fc p) j -> p fc j", p=128),
                               dbg=f"ldW2{e}{dc2}")
                    bank = 2 + dc2 % 2
                    po = PS[:, bank * 512:(bank + 1) * 512]
                    for fc in range(32):
                        bd.mm(po, abv(w2o + fc * 128, 128), abv(HSB + fc * 512, 512),
                              fc == 0, fc == 31, dbg=f"y{e}{sbk}{dc2}")
                    if b == 0 and sbk == 0:
                        rps = PS[:, 6 * 512 + 256:6 * 512 + 257]
                        for fc in range(32):
                            bd.mm(rps, abv(w2o + fc * 128, 128),
                                  rlb1[:, e * 32 + fc:e * 32 + fc + 1],
                                  fc == 0, fc == 31, dbg=f"re{e}{dc2}")
                        bd.cp(small[:, 32 + dc2:33 + dc2], rps, eng="dve",
                              dbg=f"reev{e}{dc2}")
                    bd.cp(afv(AF_T0, 512), po, eng="dve", dbg=f"yst{e}{sbk}{dc2}")
                    tb = 4 + dc2 % 2
                    for i4 in range(4):
                        bd.tp(PS[:, tb * 512 + i4 * 128:tb * 512 + (i4 + 1) * 128],
                              afv(AF_T0 + i4 * 128, 128), dbg=f"ytp{e}{sbk}{dc2}")
                    S_.op("dve",
                          (lambda tbv, dc2_: (lambda e2: e2.tensor_copy(
                              bass.AP(AFA, AF_BC0 + dc2_ * 128,
                                      [[AFN, 128], [1024, 4], [1, 128]]), tbv)))(
                              PS[:, tb * 512:(tb + 1) * 512], dc2),
                          reads=[PS[:, tb * 512:(tb + 1) * 512]],
                          writes=[afv(AF_BC0, 4096)], dbg=f"ytpe{e}{sbk}{dc2}")
                bd.dma("qpool",
                       bass.AP(ff_dr.tensor, b * S * D + sbk * 512 * D,
                               [[D, 128], [128 * D, 4], [1, D]]),
                       bass.AP(AFA, AF_BC0, [[AFN, 128], [1024, 4], [1, 1024]]),
                       accum=(e > 0), dbg=f"ffacc{e}{sbk}")
            if b == 0:
                bd.dma("qact", rcol_dr[e].rearrange("(c p) -> p c", p=128),
                       small[:, 32:40], slow=True, dbg=f"rst{e}")
                bd.dma("qact", row_buf[0:1, 0:D], rcol_dr[e].rearrange("(a d2) -> a d2", a=1),
                       dbg=f"rld{e}")
                bd.dma("qact", row_buf[0:1, D:2 * D], dt["fb2"][e].rearrange("(a d2) -> a d2", a=1),
                       dbg=f"ldb2{e}")
                bd.tt(cmat[0:1, e * D:(e + 1) * D], row_buf[0:1, 0:D],
                      row_buf[0:1, D:2 * D], OP.add, dbg=f"cm{e}")
        if b == 0:
            for e in range(NE):
                bd.dma("qact", row_buf[0:1, 0:D], rcol_dr[e].rearrange("(a d2) -> a d2", a=1),
                       dbg=f"rR{e}")
                if e == 0:
                    bd.cp(row_buf[0:1, D:2 * D], row_buf[0:1, 0:D], dbg="Rcp")
                else:
                    bd.tt(row_buf[0:1, D:2 * D], row_buf[0:1, D:2 * D],
                          row_buf[0:1, 0:D], OP.add, dbg=f"Radd{e}")
            bd.dma("qact", rrow_dr, row_buf[0:1, D:2 * D], dbg="Rst")
        bd.dma("qsp", row_buf[0:1, 0:D], rrow_dr, dbg="Rld")
        bcast_ap(AF_BC2, row_buf, 6, dbg="Rbc")
        for st in range(ST):
            for nb2 in range(2):
                po = PS[:, (6 + nb2) * 512:(7 + nb2) * 512]
                for e in range(NE):
                    bd.mm(po, mrow[0:1, e * S + st * 128:e * S + (st + 1) * 128],
                          cmat[0:1, e * D + nb2 * 512:e * D + (nb2 + 1) * 512],
                          e == 0, e == NE - 1, dbg=f"corr{st}")
                bd.tt(afv(AF_T0 + nb2 * 512, 512), po, afv(AF_BC2 + nb2 * 512, 512),
                      OP.subtract, dbg=f"correv{st}")
            bd.dma("qpool", ff_dr[b, st * 128:(st + 1) * 128, :], afv(AF_T0, D),
                   accum=True, dbg=f"corracc{st}")

        bcast_in(AF_BC0, dt["g2"].rearrange("(a d2) -> a d2", a=1), 6, dbg="g2")
        bcast_in(AF_BC1, dt["be2"].rearrange("(a d2) -> a d2", a=1), 6, dbg="be2")
        bcast_in(AF_RB, dt["abo"].rearrange("(a d2) -> a d2", a=1), 6, dbg="boa")
        for st in range(ST):
            ta = AF_T0 if st % 2 == 0 else AF_BC2
            tb2 = AF_T1
            sc = 8 * (st % 2)
            bd.dma("qsp", afv(ta, D), ff_dr[b, st * 128:(st + 1) * 128, :], dbg="ldff")
            bd.dma("qsp", afv(tb2, D), x1_dr[b, st * 128:(st + 1) * 128, :], dbg="ldx18")
            bd.tt(afv(ta, D), afv(ta, D), afv(tb2, D), OP.add, dbg="res2")
            lnorm_tok(ta, tb2, sc=sc)
            bd.tt(afv(ta, D), afv(ta, D), afv(AF_BC0, D), OP.mult, dbg="ln2g")
            bd.tt(afv(ta, D), afv(ta, D), afv(AF_BC1, D), OP.add, dbg="ln2b")
            bd.actf(afv(ta, D), afv(ta, D), AF.Silu, dbg="silu")
            bd.dma("qsp", afv(tb2, D), sp_dr[b, st * 128:(st + 1) * 128, :], dbg="ldsp")
            bd.tt(afv(tb2, D), afv(tb2, D), afv(AF_RB, D), OP.add, dbg="sbo")
            bd.tt(afv(ta, D), afv(ta, D), afv(tb2, D), OP.mult, dbg="sg")
            bd.dma("qsp", afv(tb2, D), x_in[b, st * 128:(st + 1) * 128, :], dbg="ldx8")
            bd.tt(afv(ta, D), afv(ta, D), afv(tb2, D), OP.add, dbg="oadd")
            bd.dma("qsp", y_out[b, st * 128:(st + 1) * 128, :], afv(ta, D), dbg="stout")

    if debug:
        for nm, t, shp in [("dbg_xn", xn_dr, [NB, S, D]), ("dbg_sp", sp_dr, [NB, S, D]),
                           ("dbg_x1", x1_dr, [NB, S, D]), ("dbg_ff", ff_dr, [NB, S, D])]:
            o = nc.dram_tensor(nm, shp, F32, kind="ExternalOutput").ap()
            bd.dma("qsp", o, t, dbg="dbg" + nm)
        ou = nc.dram_tensor("dbg_u", [NB, H, 128, D], F32, kind="ExternalOutput").ap()
        bd.dma("qpool", ou, u_dr, dbg="dbgu")
        oatt = nc.dram_tensor("dbg_att", [16, 16], F32, kind="ExternalOutput").ap()
        bd.dma("qsp", oatt, att_sb[:, :], dbg="dbgatt")
    S_.op("sp", lambda e: e.nop(), reads=[y_out], writes=[], dbg="endbar")

    with nc.Block() as block:
        S_.finish(nc, block)
    ctx.close()
    return nc


_CACHE = {}


def kernel(**inputs):
    xs = np.ascontiguousarray(np.asarray(inputs["L0_attn_x"], dtype=np.float32))
    if "prog" not in _CACHE:
        _CACHE["prog"] = build_core()
    nc = _CACHE["prog"]

    def g(k):
        return np.ascontiguousarray(np.asarray(inputs[k], dtype=np.float32))

    base = {
        "aWq": g("L0_attn_Wq"), "aWk": g("L0_attn_Wk"), "aWv": g("L0_attn_Wv"),
        "aWo": g("L0_attn_Wo"), "abo": g("L0_attn_bo"),
        "fWq": g("L0_ffn_Wq"), "fbq": g("L0_ffn_bq"), "fWk": g("L0_ffn_Wk"),
        "fbk": g("L0_ffn_bk"), "fWv": g("L0_ffn_Wv"), "fbv": g("L0_ffn_bv"),
        "fWo": g("L0_ffn_Wo"), "fbo": g("L0_ffn_bo"),
        "frw": g("L0_ffn_router_w"), "frb": g("L0_ffn_router_b"),
        "fW1": g("L0_ffn_W1"), "fb1": g("L0_ffn_b1"),
        "fW2": g("L0_ffn_W2"), "fb2": g("L0_ffn_b2"),
        "g1": g("L0_ffn_ln1_g"), "be1": g("L0_ffn_ln1_b"),
        "g2": g("L0_ffn_ln2_g"), "be2": g("L0_ffn_ln2_b"),
        "ident": np.eye(128, dtype=np.float32),
        "onesf": np.ones((128, 128), dtype=np.float32),
        "onesbf": np.ones((128, 128), dtype=ml_dtypes.bfloat16),
    }
    in_maps = []
    for c in range(8):
        m = dict(base)
        m["xsh"] = xs[c * NB:(c + 1) * NB]
        in_maps.append(m)
    res = run_bass_kernel_spmd(nc, in_maps, core_ids=list(range(8)))
    return np.concatenate([r["y"] for r in res.results], axis=0)


# version 06
# speedup vs baseline: 1.0108x; 1.0108x over previous
import numpy as np
import ml_dtypes
import concourse.bass as bass
import concourse.mybir as mybir
from concourse.bass_utils import run_bass_kernel_spmd

F32 = mybir.dt.float32
BF16 = mybir.dt.bfloat16
I32 = mybir.dt.int32
AF = mybir.ActivationFunctionType
OP = mybir.AluOpType
AX = mybir.AxisListType

NB = 2
S = 2048
D = 1024
H = 16
DH = 64
FF = 4096
NE = 4
ST = S // 128
DC = D // 128
EPS = 1e-5
SLAB = 2048
ENGINES = ("pe", "act", "dve", "pool", "sp")


def _ap_region(ap):
    t = ap.tensor
    name = t.name
    dims = [list(d) for d in ap.ap]
    off = ap.offset
    kindname = type(t).__name__
    if "DRam" in kindname:
        lo = off
        hi = off
        for stp, ct in dims:
            if ct > 1:
                if stp >= 0:
                    hi += stp * (ct - 1)
                else:
                    lo += stp * (ct - 1)
        return (name, 0, 128, lo, hi + 1)
    pstep, pcnt = dims[0]
    if pstep != 0:
        p0 = off // pstep
        fo = off - p0 * pstep
    else:
        p0, fo = 0, off
    lo = fo
    hi = fo
    for stp, ct in dims[1:]:
        if ct > 1:
            if stp >= 0:
                hi += stp * (ct - 1)
            else:
                lo += stp * (ct - 1)
    base = getattr(t, "base_partition", 0) or 0
    if "PSum" in kindname:
        lo = (lo // 512) * 512
        hi = ((hi // 512) + 1) * 512 - 1
        return (name, 0, 128, lo, hi + 1)
    return (name, base + p0, base + p0 + pcnt, lo, hi + 1)


def _overlap(a, b):
    return not (a[2] <= b[1] or b[2] <= a[1] or a[4] <= b[3] or b[4] <= a[3])


def _covers(a, b):
    return a[1] <= b[1] and a[2] >= b[2] and a[3] <= b[3] and a[4] >= b[4]


class _Op:
    __slots__ = ("stream", "space", "emit", "is_dma", "sig", "deps", "rdeps",
                 "snap", "seq", "waits", "dbg", "idx")

    def __init__(self, stream, space, emit, is_dma, dbg):
        self.stream = stream
        self.space = space
        self.emit = emit
        self.is_dma = is_dma
        self.sig = False
        self.deps = []
        self.rdeps = []
        self.snap = None
        self.seq = 0
        self.waits = []
        self.dbg = dbg
        self.idx = 0


class Sched:
    def __init__(self):
        self.ops = []
        self.hist = {}

    def _slabs(self, reg):
        return range(reg[3] // SLAB, (reg[4] - 1) // SLAB + 1)

    def op(self, stream, emit, reads=(), writes=(), queue=None, dbg=""):
        if queue:
            qi = self.qidx = getattr(self, "qidx", {})
            k = qi.get(queue, 0)
            qi[queue] = k + 1
            space = f"{queue}#{k % 4}"
        else:
            space = stream
        o = _Op(stream, space, emit, queue is not None, dbg)
        deps = set()
        rregs = [_ap_region(a) for a in reads if a is not None and hasattr(a, "tensor")]
        wregs = [_ap_region(a) for a in writes if a is not None and hasattr(a, "tensor")]
        for r in rregs:
            for sl in self._slabs(r):
                h = self.hist.get((r[0], sl))
                if not h:
                    continue
                for wr, wop in reversed(h["w"]):
                    if _overlap(wr, r):
                        deps.add(wop)
                        if _covers(wr, r):
                            break
        for w in wregs:
            for sl in self._slabs(w):
                h = self.hist.get((w[0], sl))
                if not h:
                    continue
                for wr, wop in reversed(h["w"]):
                    if _overlap(wr, w):
                        deps.add(wop)
                        if _covers(wr, w):
                            break
                for rr, rop in h["r"]:
                    if _overlap(rr, w):
                        deps.add(rop)
        for r in rregs:
            for sl in self._slabs(r):
                self.hist.setdefault((r[0], sl), {"w": [], "r": []})["r"].append((r, o))
        for w in wregs:
            for sl in self._slabs(w):
                h = self.hist.setdefault((w[0], sl), {"w": [], "r": []})
                h["w"] = [e for e in h["w"] if not _covers(w, e[0])]
                h["w"].append((w, o))
                h["r"] = [e for e in h["r"] if not _covers(w, e[0])]
        deps.discard(o)
        o.deps = list(deps)
        o.idx = len(self.ops)
        self.ops.append(o)
        return o

    def finish(self, nc, block):
        for o in self.ops:
            best = {}
            for d in o.deps:
                if (not d.is_dma) and d.stream == o.stream == "pe":
                    continue
                if d.idx > best.get(d.space, (-1, None))[0]:
                    best[d.space] = (d.idx, d)
            o.rdeps = [d for _, d in best.values()]
            for d in o.rdeps:
                d.sig = True
        for o in self.ops:
            if o.is_dma:
                o.sig = True
        counts = {}
        for o in self.ops:
            if o.sig:
                step = 16 if o.is_dma else 1
                counts[o.space] = counts.get(o.space, 0) + step
                o.seq = counts[o.space]
        clocks = {e: {} for e in ENGINES}
        for o in self.ops:
            ck = clocks[o.stream]
            pend = {}
            psnap = {}
            if o.is_dma and o.seq > 16:
                pend[o.space] = o.seq - 16
                psnap[o.space] = None
            for d in o.rdeps:
                if d.seq > pend.get(d.space, 0):
                    pend[d.space] = d.seq
                    psnap[d.space] = d.snap
            for k in sorted(pend):
                v = pend[k]
                if ck.get(k, 0) >= v:
                    continue
                o.waits.append((k, v))
                ck[k] = v
                sn = psnap[k]
                if sn:
                    for k2, v2 in sn.items():
                        if ck.get(k2, 0) < v2:
                            ck[k2] = v2
            if o.sig:
                snap = dict(ck)
                snap[o.space] = max(snap.get(o.space, 0), o.seq)
                o.snap = snap
        all_spaces = set(counts.keys())
        for o in self.ops:
            all_spaces.add(o.space)
        sems = {}
        for sp_name in sorted(all_spaces):
            sems[sp_name] = nc.semaphore(f"sem_{sp_name}").__enter__()
        per_stream = {e: [] for e in ENGINES}
        for o in self.ops:
            per_stream[o.stream].append(o)

        def emit_stream(eng, stream):
            for o in per_stream[stream]:
                for s, v in o.waits:
                    eng.wait_ge(sems[s], v)
                instr = o.emit(eng)
                if o.sig:
                    instr.then_inc(sems[o.space], 16 if o.is_dma else 1)

        @block.tensor
        def _(eng):
            emit_stream(eng, "pe")

        @block.scalar
        def _(eng):
            emit_stream(eng, "act")

        @block.vector
        def _(eng):
            emit_stream(eng, "dve")

        @block.gpsimd
        def _(eng):
            emit_stream(eng, "pool")

        @block.sync
        def _(eng):
            emit_stream(eng, "sp")


ABN = 73984
XNT = 0
ATP = 0
X1M = 0
W2B = 4096
MBB = 12288
QA = 16384
KA = 32768
W1B = 16384
VAT = 49152
WB6 = 49152
HSB = 49152
UST = 65792
WB0 = 65792
PT0 = 65792
X1SL = 65536
AFN = 6144
AF_T0 = 0
AF_T1 = 1024
AF_BC0 = 2048
AF_BC1 = 3072
AF_BC2 = 4096
AF_RB = 5120


def build_core(debug=False):
    import contextlib
    nc = bass.Bass("TRN2", target_bir_lowering=False)
    ctx = contextlib.ExitStack()
    S_ = Sched()

    dt = {}

    def din(name, shape, dtype=F32):
        dt[name] = nc.dram_tensor(name, shape, dtype, kind="ExternalInput").ap()
        return dt[name]

    x_in = din("xsh", [NB, S, D])
    for n, shp in [("aWq", [D, D]), ("aWk", [D, D]), ("aWv", [D, D]), ("aWo", [D, D]),
                   ("abo", [D]), ("fWq", [D, D]), ("fbq", [D]), ("fWk", [D, D]),
                   ("fbk", [D]), ("fWv", [D, D]), ("fbv", [D]), ("fWo", [D, D]),
                   ("fbo", [D]), ("frw", [D, 1]), ("frb", [1]),
                   ("fW1", [NE, D, FF]), ("fb1", [NE, FF]),
                   ("fW2", [NE, FF, D]), ("fb2", [NE, D]),
                   ("g1", [D]), ("be1", [D]), ("g2", [D]), ("be2", [D])]:
        din(n, shp)
    din("ident", [128, 128])
    din("onesf", [128, 128])
    din("onesbf", [128, 128], BF16)

    y_out = nc.dram_tensor("y", [NB, S, D], F32, kind="ExternalOutput").ap()

    xn_dr = nc.dram_tensor("xn_dr", [NB, S, D], F32, kind="Internal").ap()
    x1_dr = nc.dram_tensor("x1_dr", [NB, S, D], F32, kind="Internal").ap()
    x1t_dr = nc.dram_tensor("x1t_dr", [NB, D, S], BF16, kind="Internal").ap()
    u_dr = nc.dram_tensor("u_dr", [NB, H, 128, D], BF16, kind="Internal").ap()
    sp_dr = nc.dram_tensor("sp_dr", [NB, S, D], F32, kind="Internal").ap()
    ff_dr = nc.dram_tensor("ff_dr", [NB, S, D], F32, kind="Internal").ap()
    rcol_dr = nc.dram_tensor("rcol_dr", [NE, D], F32, kind="Internal").ap()
    rrow_dr = nc.dram_tensor("rrow_dr", [1, D], F32, kind="Internal").ap()

    AB = ctx.enter_context(nc.sbuf_tensor("AB", [128, ABN], BF16))
    AFA = ctx.enter_context(nc.sbuf_tensor("AFar", [128, AFN], F32))
    PS = ctx.enter_context(nc.psum_tensor("PS", [128, 4096], F32))

    def sbt(name, shape, dtype):
        return ctx.enter_context(nc.sbuf_tensor(name, shape, dtype))

    ident = sbt("identsb", [128, 128], F32)
    onesf = sbt("onesfsb", [128, 128], F32)
    onesb = sbt("onesbsb", [128, 128], BF16)
    bqc = sbt("bqc", [128, DC], F32)
    bkc = sbt("bkc", [128, DC], F32)
    b1c = sbt("b1c", [128, 32 * NE], F32)
    rlb1 = sbt("rlb1", [128, 32 * NE], BF16)
    row_buf = sbt("row_buf", [1, 2 * D], F32)
    cmat = sbt("cmat", [1, NE * D], BF16)
    mrow = sbt("mrow", [1, NE * S], BF16)
    small = sbt("small", [128, 80], F32)
    smalli = sbt("smalli", [128, 2], I32)
    en_sb = sbt("en_sb", [16, 16], F32)
    att_sb = sbt("att_sb", [16, 16], F32)
    attT = sbt("attT", [16, 16], BF16)
    x1tstg = sbt("x1tstg", [128, 1024], BF16)

    ID = ident[:, :]

    class BD:
        def mm(self, out, lhsT, rhs, start, stop, dbg=""):
            S_.op("pe", lambda e: e.matmul(out, lhsT, rhs, start=start, stop=stop),
                  reads=[lhsT, rhs], writes=[out], dbg=dbg)

        def tp(self, out, in_, dbg="", p=128):
            idv = ident[0:p, 0:p]
            S_.op("pe", lambda e: e.transpose(out, in_, idv),
                  reads=[in_, idv], writes=[out], dbg=dbg)

        def actf(self, out, in_, func, bias=None, scale=None, accum=None, dbg=""):
            kw = {}
            if bias is not None:
                kw["bias"] = bias
            if scale is not None:
                kw["scale"] = scale
            if accum is not None:
                kw["accum_out"] = accum
            S_.op("act", lambda e: e.activation(out, in_, func, **kw),
                  reads=[in_, bias if hasattr(bias, "tensor") else None],
                  writes=[out, accum], dbg=dbg)

        def tt(self, out, a, b2, op, dbg=""):
            S_.op("dve", lambda e: e.tensor_tensor(out, a, b2, op),
                  reads=[a, b2], writes=[out], dbg=dbg)

        def ts(self, out, in0, s1, s2, op0, op1=None, accum=None, dbg=""):
            kw = {"accum_out": accum} if accum is not None else {}
            if op1 is not None:
                S_.op("dve", lambda e: e.tensor_scalar(out, in0, s1, s2, op0, op1, **kw),
                      reads=[in0, s1 if hasattr(s1, "tensor") else None,
                             s2 if hasattr(s2, "tensor") else None],
                      writes=[out, accum], dbg=dbg)
            else:
                S_.op("dve", lambda e: e.tensor_scalar(out, in0, s1, s2, op0, **kw),
                      reads=[in0, s1 if hasattr(s1, "tensor") else None],
                      writes=[out, accum], dbg=dbg)

        def stt(self, out, in0, scalar, in1, op0, op1, dbg=""):
            S_.op("dve", lambda e: e.scalar_tensor_tensor(out, in0, scalar, in1, op0, op1),
                  reads=[in0, in1], writes=[out], dbg=dbg)

        def cp(self, out, in_, eng="dve", dbg=""):
            if eng == "act":
                S_.op(eng, lambda e: e.copy(out, in_), reads=[in_], writes=[out], dbg=dbg)
            else:
                S_.op(eng, lambda e: e.tensor_copy(out, in_), reads=[in_], writes=[out], dbg=dbg)

        def cpa(self, out_ap, in_ap, wr_hint, eng="dve", dbg=""):
            if eng == "act":
                S_.op(eng, lambda e: e.copy(out_ap, in_ap),
                      reads=[in_ap], writes=[wr_hint], dbg=dbg)
            else:
                S_.op(eng, lambda e: e.tensor_copy(out_ap, in_ap),
                      reads=[in_ap], writes=[wr_hint], dbg=dbg)

        def red(self, out, in_, op, dbg=""):
            S_.op("dve", lambda e: e.tensor_reduce(out, in_, AX.X, op),
                  reads=[in_], writes=[out], dbg=dbg)

        def rcp(self, out, in_, dbg=""):
            S_.op("dve", lambda e: e.reciprocal(out, in_), reads=[in_], writes=[out], dbg=dbg)

        def mems(self, ap, val, wr_hint=None, dbg=""):
            S_.op("dve", lambda e: e.memset(ap, val), reads=[],
                  writes=[wr_hint if wr_hint is not None else ap], dbg=dbg)

        def dma(self, queue, out, in_, accum=False, slow=False, dbg=""):
            eng = {"qsp": "sp", "qact": "act", "qpool": "pool"}[queue]
            kw = {}
            if accum:
                kw["accum_op"] = OP.add
            if slow:
                kw["allow_slow_non_contiguous"] = True
            S_.op(eng, lambda e: e.dma_start(out=out, in_=in_, **kw),
                  reads=[in_, out if accum else None], writes=[out],
                  queue=queue, dbg=dbg)

    bd = BD()

    def abv(off, n, p0=0, pn=128):
        return AB[p0:p0 + pn, off:off + n]

    def afv(off, n, p0=0, pn=128):
        return AFA[p0:p0 + pn, off:off + n]

    def load_wDD(buf_off, wname):
        bd.dma("qpool",
               bass.AP(AB, buf_off, [[ABN, 128], [1024, 8], [1, 1024]]),
               dt[wname].rearrange("(c p) j -> p c j", p=128), dbg=f"ld{wname}")

    def bcast_ap(dst_off, src_row_ap, psbank, n=D, dbg="bc"):
        for nb2 in range(n // 512):
            po = PS[:, psbank * 512:(psbank + 1) * 512]
            bd.mm(po, onesf[0:1, 0:128], src_row_ap[0:1, nb2 * 512:(nb2 + 1) * 512],
                  True, True, dbg=dbg + "mm")
            bd.cp(afv(dst_off + nb2 * 512, 512), po, eng="dve", dbg=dbg + "ev")

    def bcast_in(dst_off, dram_row_ap, psbank, dbg="bci"):
        bd.dma("qsp", row_buf[0:1, 0:D], dram_row_ap, dbg=dbg + "ld")
        bcast_ap(dst_off, row_buf, psbank, dbg=dbg)

    def lnorm_tok(xt, disc, sc=0):
        t = afv(xt, D)
        bd.red(small[:, sc:sc + 1], t, OP.add, dbg="ln_sum")
        bd.ts(small[:, sc + 1:sc + 2], small[:, sc:sc + 1], 1.0 / D, None, OP.mult, dbg="ln_mu")
        bd.ts(t, t, small[:, sc + 1:sc + 2], None, OP.subtract, dbg="ln_sub")
        bd.actf(afv(disc, D), t, AF.Square, accum=small[:, sc + 2:sc + 3], dbg="ln_sq")
        bd.actf(small[:, sc + 3:sc + 4], small[:, sc + 2:sc + 3], AF.Sqrt,
                bias=small[:, 41:42], scale=1.0 / D, dbg="ln_sd")
        bd.rcp(small[:, sc + 4:sc + 5], small[:, sc + 3:sc + 4], dbg="ln_rcp")
        bd.ts(t, t, small[:, sc + 4:sc + 5], None, OP.mult, dbg="ln_nrm")

    bd.dma("qsp", ident[:, :], dt["ident"])
    bd.dma("qsp", onesf[:, :], dt["onesf"])
    bd.dma("qsp", onesb[:, :], dt["onesbf"])
    bd.dma("qsp", bqc[:, :], dt["fbq"].rearrange("(c p) -> p c", p=128), slow=True)
    bd.dma("qsp", bkc[:, :], dt["fbk"].rearrange("(c p) -> p c", p=128), slow=True)
    bd.dma("qsp", b1c[:, :], dt["fb1"].rearrange("e (c p) -> p (e c)", p=128), slow=True)
    bd.actf(rlb1[:, :], b1c[:, :], AF.Relu, dbg="relub1")
    bd.mems(small[:, 41:42], float(EPS), dbg="epscol")
    bd.dma("qsp", row_buf[0:1, 0:1], dt["frb"].rearrange("(a o) -> a o", a=1))
    po = PS[:, 7 * 512:7 * 512 + 1]
    bd.mm(po, onesf[0:1, 0:128], row_buf[0:1, 0:1], True, True, dbg="frbmm")
    bd.cp(small[:, 40:41], po, eng="dve", dbg="frbev")

    for b in range(NB):
        for st in range(ST):
            xt = AF_T0 if st % 2 == 0 else AF_T1
            bd.dma("qsp", afv(xt, D), x_in[b, st * 128:(st + 1) * 128, :], dbg=f"ldx{st}")
            lnorm_tok(xt, AF_BC1 if st % 2 == 0 else AF_BC2, sc=8 * (st % 2))
            bd.dma("qsp", xn_dr[b, st * 128:(st + 1) * 128, :], afv(xt, D), dbg=f"stxn{st}")
            for g in range(2):
                bank = (st % 2) * 2 + g
                for c4 in range(4):
                    c = g * 4 + c4
                    bd.tp(PS[:, bank * 512 + c4 * 128: bank * 512 + (c4 + 1) * 128],
                          afv(xt + c * 128, 128), dbg=f"tpxn{st}_{c}")
                pob = PS[:, bank * 512:(bank + 1) * 512]
                bd.cpa(bass.AP(AB, XNT + g * 4 * 2048 + st * 128,
                               [[ABN, 128], [2048, 4], [1, 128]]),
                       pob, abv(XNT + g * 4 * 2048 + st * 128, 3 * 2048 + 128),
                       eng=("dve" if st % 2 else "act"), dbg=f"evxnT{st}_{g}")

        load_wDD(WB0, "aWq")
        for (dst, tag) in ((QA, "q"), (KA, "k")):
            for st in range(ST):
                for nb2 in range(2):
                    bank = (st * 2 + nb2) % 4
                    po = PS[:, bank * 512:(bank + 1) * 512]
                    for c in range(DC):
                        bd.mm(po, abv(XNT + c * 2048 + st * 128, 128),
                              abv(WB0 + c * 1024 + nb2 * 512, 512),
                              c == 0, c == DC - 1, dbg=f"{tag}proj")
                    bd.cp(abv(dst + st * 1024 + nb2 * 512, 512), po,
                          eng=("act" if (st + nb2) % 2 else "dve"), dbg=f"{tag}ev")
            if dst == QA:
                load_wDD(WB0, "aWk")
        en_ps = PS[0:16, 4 * 512:4 * 512 + 16]
        first = True
        for d in range(DH):
            for st in range(ST):
                qv = bass.AP(AB, QA + st * 1024 + d, [[ABN, 128], [64, 16]])
                kv = bass.AP(AB, KA + st * 1024 + d, [[ABN, 128], [64, 16]])
                bd.mm(en_ps, qv, kv, first, (d == DH - 1 and st == ST - 1), dbg="energy")
                first = False
        load_wDD(WB0, "aWv")
        for co in range(DC):
            for sb4 in range(4):
                bank = (co * 4 + sb4) % 4
                po = PS[:, bank * 512:(bank + 1) * 512]
                for c in range(DC):
                    bd.mm(po, abv(WB0 + c * 1024 + co * 128, 128),
                          abv(XNT + c * 2048 + sb4 * 512, 512),
                          c == 0, c == DC - 1, dbg="vaT")
                bd.cp(abv(VAT + co * 2048 + sb4 * 512, 512), po,
                      eng=("act" if (co + sb4) % 2 else "dve"), dbg="vaTev")
        bd.red(small[0:16, 5:6], en_ps, OP.max, dbg="en_max")
        bd.ts(small[0:16, 6:7], small[0:16, 5:6], -0.125, None, OP.mult, dbg="en_nm")
        bd.actf(en_sb[:, :], en_ps, AF.Exp, bias=small[0:16, 6:7], scale=0.125,
                accum=small[0:16, 7:8], dbg="en_exp")
        bd.rcp(small[0:16, 8:9], small[0:16, 7:8], dbg="en_rcp")
        bd.ts(att_sb[:, :], en_sb[:, :], small[0:16, 8:9], None, OP.mult, dbg="en_att")
        attT_ps = PS[0:16, 5 * 512:5 * 512 + 16]
        bd.tp(attT_ps, att_sb[:, :], dbg="attT", p=16)
        bd.cp(attT[:, :], attT_ps, eng="dve", dbg="attTev")

        for t in range(16):
            bd.dma("qpool", abv(QA + t * 1024, 1024, 0, 64),
                   dt["aWo"][64 * t:64 * t + 64, :], dbg="woA")
            bd.dma("qpool", abv(KA + t * 1024, 1024, 64, 64),
                   dt["aWo"][64 * t:64 * t + 64, :], dbg="woB")
        for bb in range(16):
            pb = 64 * (bb % 2)
            wo = QA if pb == 0 else KA
            for nb2 in range(2):
                po = PS[:, nb2 * 512:(nb2 + 1) * 512]
                for j in range(8):
                    for half in range(2):
                        t = 2 * j + half
                        lv = bass.AP(AB, pb * ABN + VAT + (bb // 2) * 2048 + t,
                                     [[ABN, 64], [16, 128]])
                        rv = abv(wo + t * 1024 + nb2 * 512, 512, pb, 64)
                        bd.mm(po, lv, rv, (j == 0 and half == 0), (j == 7 and half == 1),
                              dbg=f"u{bb}")
                bd.cp(abv(UST + (bb % 2) * 1024 + nb2 * 512, 512), po,
                      eng=("act" if nb2 else "dve"), dbg="uev")
            bd.dma("qact", u_dr[b, bb], abv(UST + (bb % 2) * 1024, 1024), dbg=f"ust{bb}")

        spst0 = AFA[0:16, AF_BC0:AF_BC0 + 2048]
        spst1 = AFA[0:16, AF_BC2:AF_BC2 + 2048]
        for pi in range(8):
            uoff = QA if pi % 2 == 0 else KA
            bd.dma("qpool", abv(uoff, 16384, 0, 16),
                   u_dr[b, :, 16 * pi:16 * (pi + 1), :].rearrange("h r d2 -> h (r d2)"),
                   dbg=f"ldU{pi}")
            for qq in range(8):
                bset = 0 if qq % 2 == 0 else 4
                for k4 in range(4):
                    bd.mm(PS[0:16, (bset + k4) * 512:(bset + k4) * 512 + 512],
                          attT[:, :], abv(uoff + qq * 2048 + k4 * 512, 512, 0, 16),
                          True, True, dbg="spmm")
                pov = PS[0:16, bset * 512:(bset + 4) * 512]
                dstv = spst0 if qq % 2 == 0 else spst1
                bd.cp(dstv, pov, eng=("act" if qq % 2 else "dve"), dbg="spev")
                bd.dma("qsp",
                       sp_dr[b].rearrange("(h r) d2 -> h (r d2)", h=16)[
                           :, pi * 16384 + qq * 2048: pi * 16384 + (qq + 1) * 2048],
                       dstv, dbg="spst")

        bcast_in(AF_BC0, dt["fbv"].rearrange("(a d2) -> a d2", a=1), 6, dbg="bv")
        load_wDD(WB0, "fWq")
        for (wdst, bcol, tag) in ((QA, bqc, "qb"), (KA, bkc, "kb")):
            for co in range(DC):
                for sb4 in range(4):
                    bank = (co * 4 + sb4) % 4
                    po = PS[:, bank * 512:(bank + 1) * 512]
                    for c in range(DC):
                        bd.mm(po, abv(WB0 + c * 1024 + co * 128, 128),
                              abv(XNT + c * 2048 + sb4 * 512, 512),
                              c == 0, c == DC - 1, dbg=tag)
                    bd.ts(abv(wdst + co * 2048 + sb4 * 512, 512), po,
                          bcol[:, co:co + 1], None, OP.add, dbg=tag + "ev")
            if wdst == QA:
                load_wDD(WB0, "fWk")
        load_wDD(WB0, "fWv")
        for st in range(ST):
            for nb2 in range(2):
                bank = 4 + (st * 2 + nb2) % 2
                po = PS[:, bank * 512:(bank + 1) * 512]
                for c in range(DC):
                    bd.mm(po, abv(XNT + c * 2048 + st * 128, 128),
                          abv(WB0 + c * 1024 + nb2 * 512, 512),
                          c == 0, c == DC - 1, dbg="vaug")
                S_.op("dve",
                      (lambda po_, st_, nb2_: (lambda e: e.tensor_tensor(
                          bass.AP(AB, VAT + st_ * 1040 + nb2_ * 8 * 65,
                                  [[ABN, 128], [65, 8], [1, 64]]),
                          po_,
                          bass.AP(AFA, AF_BC0 + nb2_ * 512,
                                  [[AFN, 128], [64, 8], [1, 64]]),
                          OP.add)))(po, st, nb2),
                      reads=[po, afv(AF_BC0 + nb2 * 512, 512)],
                      writes=[abv(VAT + st * 1040 + nb2 * 520, 524)], dbg="vaugev")
            S_.op("dve",
                  (lambda st_: (lambda e: e.memset(
                      bass.AP(AB, VAT + st_ * 1040 + 64, [[ABN, 128], [65, 16], [1, 1]]),
                      1.0)))(st),
                  reads=[], writes=[abv(VAT + st * 1040, 1040)], dbg="vones")

        for h in range(H):
            hb = 64 * (h % 2)
            hc = h // 2
            for kt in range(ST):
                ptb = PT0 + (kt % 4) * 2048
                for qp in range(2):
                    bp = qp * 2
                    for qh in range(2):
                        qb4 = qp * 2 + qh
                        bd.mm(PS[:, (bp + qh) * 512:(bp + qh + 1) * 512],
                              bass.AP(AB, hb * ABN + KA + hc * 2048 + kt * 128,
                                      [[ABN, 64], [1, 128]]),
                              abv(QA + hc * 2048 + qb4 * 512, 512, hb, 64),
                              True, True, dbg=f"sc{h}")
                    bd.actf(abv(ptb + qp * 1024, 1024),
                            PS[:, bp * 512:(bp + 2) * 512], AF.Exp, scale=0.125,
                            dbg=f"exp{h}")
                for qb4 in range(4):
                    bd.mm(PS[0:65, (4 + qb4) * 512:(4 + qb4) * 512 + 512],
                          abv(VAT + kt * 1040 + 65 * h, 65),
                          abv(ptb + qb4 * 512, 512),
                          kt == 0, kt == ST - 1, dbg=f"pv{h}")
            for qb4 in range(4):
                den = PS[64:65, (4 + qb4) * 512:(4 + qb4) * 512 + 512]
                rrow = AFA[0:1, AF_BC0 + qb4 * 512:AF_BC0 + (qb4 + 1) * 512]
                bd.rcp(rrow, den, dbg=f"den{h}")
                bc = PS[0:64, qb4 * 512:(qb4 + 1) * 512]
                bd.mm(bc, onesf[0:1, 0:64], rrow, True, True, dbg=f"bcd{h}")
                bd.cp(afv(AF_BC2 + (qb4 % 2) * 512, 512, 0, 64), bc, eng="act",
                      dbg=f"bcev{h}")
                S_.op("dve",
                      (lambda h_, hb_, hc_, qb4_: (lambda e: e.tensor_tensor(
                          bass.AP(AB, hb_ * ABN + ATP + hc_ * 2048 + qb4_ * 512,
                                  [[ABN, 64], [1, 512]]),
                          PS[0:64, (4 + qb4_) * 512:(4 + qb4_) * 512 + 512],
                          afv(AF_BC2 + (qb4_ % 2) * 512, 512, 0, 64),
                          OP.mult)))(h, hb, hc, qb4),
                      reads=[PS[0:64, (4 + qb4) * 512:(4 + qb4) * 512 + 512],
                             afv(AF_BC2 + (qb4 % 2) * 512, 512, 0, 64)],
                      writes=[abv(ATP + hc * 2048 + qb4 * 512, 512, hb, 64)],
                      dbg=f"fin{h}")

        load_wDD(WB6, "fWo")
        bcast_in(AF_BC0, dt["fbo"].rearrange("(a d2) -> a d2", a=1), 6, dbg="bo")
        bcast_in(AF_BC1, dt["g1"].rearrange("(a d2) -> a d2", a=1), 6, dbg="g1")
        bcast_in(AF_BC2, dt["be1"].rearrange("(a d2) -> a d2", a=1), 6, dbg="be1")
        bcast_in(AF_RB, dt["frw"].rearrange("d2 o -> o d2"), 6, dbg="rw")
        for st in range(ST):
            for nb2 in range(2):
                bank = (st * 2 + nb2) % 2
                po = PS[:, bank * 512:(bank + 1) * 512]
                for c in range(DC):
                    bd.mm(po, abv(ATP + c * 2048 + st * 128, 128),
                          abv(WB6 + c * 1024 + nb2 * 512, 512),
                          c == 0, c == DC - 1, dbg="attout")
                bd.tt(afv(AF_T0 + nb2 * 512, 512), po, afv(AF_BC0 + nb2 * 512, 512),
                      OP.add, dbg="attbo")
            bd.dma("qsp", afv(AF_T1, D), xn_dr[b, st * 128:(st + 1) * 128, :], dbg="ldxn6")
            bd.tt(afv(AF_T0, D), afv(AF_T0, D), afv(AF_T1, D), OP.add, dbg="res1")
            lnorm_tok(AF_T0, AF_T1)
            bd.tt(afv(AF_T0, D), afv(AF_T0, D), afv(AF_BC1, D), OP.mult, dbg="ln1g")
            bd.tt(afv(AF_T0, D), afv(AF_T0, D), afv(AF_BC2, D), OP.add, dbg="ln1b")
            bd.dma("qsp", x1_dr[b, st * 128:(st + 1) * 128, :], afv(AF_T0, D), dbg="stx1")
            bd.tt(afv(AF_T1, D), afv(AF_T0, D), afv(AF_RB, D), OP.mult, dbg="keymul")
            bd.red(small[:, 16:17], afv(AF_T1, D), OP.add, dbg="keysum")
            bd.tt(small[:, 16:17], small[:, 16:17], small[:, 40:41], OP.add, dbg="keyb")
            kc = small[:, 16:17]
            bd.cp(smalli[:, 0:1], kc, dbg="fli")
            bd.cp(small[:, 17:18], smalli[:, 0:1], dbg="flf")
            bd.tt(small[:, 18:19], small[:, 17:18], kc, OP.is_gt, dbg="flg")
            bd.tt(small[:, 19:20], small[:, 17:18], small[:, 18:19], OP.subtract, dbg="fl")
            bd.ts(small[:, 20:21], small[:, 19:20], 0.25, None, OP.mult, dbg="fq")
            bd.cp(smalli[:, 1:2], small[:, 20:21], dbg="fqi")
            bd.cp(small[:, 21:22], smalli[:, 1:2], dbg="fqf")
            bd.tt(small[:, 22:23], small[:, 21:22], small[:, 20:21], OP.is_gt, dbg="fqg")
            bd.tt(small[:, 23:24], small[:, 21:22], small[:, 22:23], OP.subtract, dbg="flq")
            bd.stt(small[:, 24:25], small[:, 23:24], -4.0, small[:, 19:20],
                   OP.mult, OP.add, dbg="mmod")
            for e in range(NE):
                bd.ts(small[:, 26 + e:27 + e], small[:, 24:25], float(e), None,
                      OP.is_equal, dbg=f"mc{e}")
                bd.tp(PS[0:1, 6 * 512 + e * 128:6 * 512 + (e + 1) * 128],
                      small[:, 26 + e:27 + e], dbg=f"mtp{e}")
            for e in range(NE):
                bd.cp(mrow[0:1, e * S + st * 128:e * S + (st + 1) * 128],
                      PS[0:1, 6 * 512 + e * 128:6 * 512 + (e + 1) * 128],
                      eng="dve", dbg=f"mrow{e}")
            for g in range(2):
                bank = 2 + g
                for c4 in range(4):
                    c = g * 4 + c4
                    bd.tp(PS[:, bank * 512 + c4 * 128:bank * 512 + (c4 + 1) * 128],
                          afv(AF_T0 + c * 128, 128), dbg=f"tpx1{st}_{c}")
                bd.cp(x1tstg[:, g * 512:(g + 1) * 512],
                      PS[:, bank * 512:(bank + 1) * 512],
                      eng=("dve" if st % 2 else "act"), dbg=f"evx1{st}_{g}")
            bd.dma("qact",
                   bass.AP(x1t_dr.tensor, b * D * S + st * 128,
                           [[S, 128], [128 * S, 8], [1, 128]]),
                   bass.AP(x1tstg, 0, [[1024, 128], [128, 8], [1, 128]]),
                   dbg=f"stx1t{st}")

        for e in range(NE):
            bd.dma("qpool",
                   bass.AP(AB, W1B, [[ABN, 128], [4096, 8], [1, 4096]]),
                   dt["fW1"][e].rearrange("(c p) f -> p c f", p=128), dbg=f"ldW1{e}")
            for sbk in range(4):
                xsl = X1SL + (sbk % 2) * 4096
                bd.dma("qpool",
                       bass.AP(AB, xsl, [[ABN, 128], [512, 8], [1, 512]]),
                       x1t_dr[b].rearrange("(c p) s2 -> p c s2", p=128)[
                           :, :, sbk * 512:(sbk + 1) * 512],
                       dbg=f"ldx1t{e}{sbk}")
                pom = PS[:, 6 * 512:7 * 512]
                bd.mm(pom, onesb[0:1, 0:128],
                      mrow[0:1, e * S + sbk * 512:e * S + (sbk + 1) * 512],
                      True, True, dbg=f"mb{e}{sbk}")
                bd.cp(abv(MBB, 512), pom, eng="dve", dbg=f"mbev{e}{sbk}")
                for c in range(DC):
                    bd.tt(abv(X1M + c * 512, 512), abv(xsl + c * 512, 512),
                          abv(MBB, 512), OP.mult, dbg=f"x1m{e}{sbk}")
                for fc in range(32):
                    bank = fc % 4
                    po = PS[:, bank * 512:(bank + 1) * 512]
                    for c in range(DC):
                        bd.mm(po, abv(W1B + c * 4096 + fc * 128, 128),
                              abv(X1M + c * 512, 512), c == 0, c == DC - 1,
                              dbg=f"h{e}{sbk}")
                    bd.actf(abv(HSB + fc * 512, 512), po, AF.Relu,
                            bias=b1c[:, e * 32 + fc:e * 32 + fc + 1],
                            dbg=f"hrl{e}{sbk}")
                for dc2 in range(DC):
                    w2o = W2B + (dc2 % 2) * 4096
                    for hf in range(2):
                        bd.dma("qpool",
                               bass.AP(AB, w2o + hf * 2048, [[ABN, 128], [128, 16], [1, 128]]),
                               dt["fW2"][e, hf * 2048:(hf + 1) * 2048,
                                         dc2 * 128:(dc2 + 1) * 128].rearrange(
                                   "(fc p) j -> p fc j", p=128),
                               dbg=f"ldW2{e}{dc2}")
                    bank = 2 + dc2 % 2
                    po = PS[:, bank * 512:(bank + 1) * 512]
                    for fc in range(32):
                        bd.mm(po, abv(w2o + fc * 128, 128), abv(HSB + fc * 512, 512),
                              fc == 0, fc == 31, dbg=f"y{e}{sbk}{dc2}")
                    if b == 0 and sbk == 0:
                        rps = PS[:, 6 * 512 + 256:6 * 512 + 257]
                        for fc in range(32):
                            bd.mm(rps, abv(w2o + fc * 128, 128),
                                  rlb1[:, e * 32 + fc:e * 32 + fc + 1],
                                  fc == 0, fc == 31, dbg=f"re{e}{dc2}")
                        bd.cp(small[:, 32 + dc2:33 + dc2], rps, eng="dve",
                              dbg=f"reev{e}{dc2}")
                    bd.cp(afv(AF_T0, 512), po, eng="dve", dbg=f"yst{e}{sbk}{dc2}")
                    tb = 4 + dc2 % 2
                    for i4 in range(4):
                        bd.tp(PS[:, tb * 512 + i4 * 128:tb * 512 + (i4 + 1) * 128],
                              afv(AF_T0 + i4 * 128, 128), dbg=f"ytp{e}{sbk}{dc2}")
                    S_.op("dve",
                          (lambda tbv, dc2_: (lambda e2: e2.tensor_copy(
                              bass.AP(AFA, AF_BC0 + dc2_ * 128,
                                      [[AFN, 128], [1024, 4], [1, 128]]), tbv)))(
                              PS[:, tb * 512:(tb + 1) * 512], dc2),
                          reads=[PS[:, tb * 512:(tb + 1) * 512]],
                          writes=[afv(AF_BC0, 4096)], dbg=f"ytpe{e}{sbk}{dc2}")
                bd.dma("qpool",
                       bass.AP(ff_dr.tensor, b * S * D + sbk * 512 * D,
                               [[D, 128], [128 * D, 4], [1, D]]),
                       bass.AP(AFA, AF_BC0, [[AFN, 128], [1024, 4], [1, 1024]]),
                       accum=(e > 0), dbg=f"ffacc{e}{sbk}")
            if b == 0:
                bd.dma("qact", rcol_dr[e].rearrange("(c p) -> p c", p=128),
                       small[:, 32:40], slow=True, dbg=f"rst{e}")
                bd.dma("qact", row_buf[0:1, 0:D], rcol_dr[e].rearrange("(a d2) -> a d2", a=1),
                       dbg=f"rld{e}")
                bd.dma("qact", row_buf[0:1, D:2 * D], dt["fb2"][e].rearrange("(a d2) -> a d2", a=1),
                       dbg=f"ldb2{e}")
                bd.tt(cmat[0:1, e * D:(e + 1) * D], row_buf[0:1, 0:D],
                      row_buf[0:1, D:2 * D], OP.add, dbg=f"cm{e}")
        if b == 0:
            for e in range(NE):
                bd.dma("qact", row_buf[0:1, 0:D], rcol_dr[e].rearrange("(a d2) -> a d2", a=1),
                       dbg=f"rR{e}")
                if e == 0:
                    bd.cp(row_buf[0:1, D:2 * D], row_buf[0:1, 0:D], dbg="Rcp")
                else:
                    bd.tt(row_buf[0:1, D:2 * D], row_buf[0:1, D:2 * D],
                          row_buf[0:1, 0:D], OP.add, dbg=f"Radd{e}")
            bd.dma("qact", rrow_dr, row_buf[0:1, D:2 * D], dbg="Rst")
        bd.dma("qsp", row_buf[0:1, 0:D], rrow_dr, dbg="Rld")
        bcast_ap(AF_BC2, row_buf, 6, dbg="Rbc")
        for st in range(ST):
            for nb2 in range(2):
                po = PS[:, (6 + nb2) * 512:(7 + nb2) * 512]
                for e in range(NE):
                    bd.mm(po, mrow[0:1, e * S + st * 128:e * S + (st + 1) * 128],
                          cmat[0:1, e * D + nb2 * 512:e * D + (nb2 + 1) * 512],
                          e == 0, e == NE - 1, dbg=f"corr{st}")
                bd.tt(afv(AF_T0 + nb2 * 512, 512), po, afv(AF_BC2 + nb2 * 512, 512),
                      OP.subtract, dbg=f"correv{st}")
            bd.dma("qpool", ff_dr[b, st * 128:(st + 1) * 128, :], afv(AF_T0, D),
                   accum=True, dbg=f"corracc{st}")

        bcast_in(AF_BC0, dt["g2"].rearrange("(a d2) -> a d2", a=1), 6, dbg="g2")
        bcast_in(AF_BC1, dt["be2"].rearrange("(a d2) -> a d2", a=1), 6, dbg="be2")
        bcast_in(AF_RB, dt["abo"].rearrange("(a d2) -> a d2", a=1), 6, dbg="boa")
        for st in range(ST):
            ta = AF_T0 if st % 2 == 0 else AF_BC2
            tb2 = AF_T1
            sc = 8 * (st % 2)
            bd.dma("qsp", afv(ta, D), ff_dr[b, st * 128:(st + 1) * 128, :], dbg="ldff")
            bd.dma("qsp", afv(tb2, D), x1_dr[b, st * 128:(st + 1) * 128, :], dbg="ldx18")
            bd.tt(afv(ta, D), afv(ta, D), afv(tb2, D), OP.add, dbg="res2")
            lnorm_tok(ta, tb2, sc=sc)
            bd.tt(afv(ta, D), afv(ta, D), afv(AF_BC0, D), OP.mult, dbg="ln2g")
            bd.tt(afv(ta, D), afv(ta, D), afv(AF_BC1, D), OP.add, dbg="ln2b")
            bd.actf(afv(ta, D), afv(ta, D), AF.Silu, dbg="silu")
            bd.dma("qsp", afv(tb2, D), sp_dr[b, st * 128:(st + 1) * 128, :], dbg="ldsp")
            bd.tt(afv(tb2, D), afv(tb2, D), afv(AF_RB, D), OP.add, dbg="sbo")
            bd.tt(afv(ta, D), afv(ta, D), afv(tb2, D), OP.mult, dbg="sg")
            bd.dma("qsp", afv(tb2, D), x_in[b, st * 128:(st + 1) * 128, :], dbg="ldx8")
            bd.tt(afv(ta, D), afv(ta, D), afv(tb2, D), OP.add, dbg="oadd")
            bd.dma("qsp", y_out[b, st * 128:(st + 1) * 128, :], afv(ta, D), dbg="stout")

    if debug:
        for nm, t, shp in [("dbg_xn", xn_dr, [NB, S, D]), ("dbg_sp", sp_dr, [NB, S, D]),
                           ("dbg_x1", x1_dr, [NB, S, D]), ("dbg_ff", ff_dr, [NB, S, D])]:
            o = nc.dram_tensor(nm, shp, F32, kind="ExternalOutput").ap()
            bd.dma("qsp", o, t, dbg="dbg" + nm)
        ou = nc.dram_tensor("dbg_u", [NB, H, 128, D], F32, kind="ExternalOutput").ap()
        bd.dma("qpool", ou, u_dr, dbg="dbgu")
        oatt = nc.dram_tensor("dbg_att", [16, 16], F32, kind="ExternalOutput").ap()
        bd.dma("qsp", oatt, att_sb[:, :], dbg="dbgatt")
    S_.op("sp", lambda e: e.nop(), reads=[y_out], writes=[], dbg="endbar")

    with nc.Block() as block:
        S_.finish(nc, block)
    ctx.close()
    return nc


_CACHE = {}


def kernel(**inputs):
    xs = np.ascontiguousarray(np.asarray(inputs["L0_attn_x"], dtype=np.float32))
    if "prog" not in _CACHE:
        _CACHE["prog"] = build_core()
    nc = _CACHE["prog"]

    def g(k):
        return np.ascontiguousarray(np.asarray(inputs[k], dtype=np.float32))

    base = {
        "aWq": g("L0_attn_Wq"), "aWk": g("L0_attn_Wk"), "aWv": g("L0_attn_Wv"),
        "aWo": g("L0_attn_Wo"), "abo": g("L0_attn_bo"),
        "fWq": g("L0_ffn_Wq"), "fbq": g("L0_ffn_bq"), "fWk": g("L0_ffn_Wk"),
        "fbk": g("L0_ffn_bk"), "fWv": g("L0_ffn_Wv"), "fbv": g("L0_ffn_bv"),
        "fWo": g("L0_ffn_Wo"), "fbo": g("L0_ffn_bo"),
        "frw": g("L0_ffn_router_w"), "frb": g("L0_ffn_router_b"),
        "fW1": g("L0_ffn_W1"), "fb1": g("L0_ffn_b1"),
        "fW2": g("L0_ffn_W2"), "fb2": g("L0_ffn_b2"),
        "g1": g("L0_ffn_ln1_g"), "be1": g("L0_ffn_ln1_b"),
        "g2": g("L0_ffn_ln2_g"), "be2": g("L0_ffn_ln2_b"),
        "ident": np.eye(128, dtype=np.float32),
        "onesf": np.ones((128, 128), dtype=np.float32),
        "onesbf": np.ones((128, 128), dtype=ml_dtypes.bfloat16),
    }
    in_maps = []
    for c in range(8):
        m = dict(base)
        m["xsh"] = xs[c * NB:(c + 1) * NB]
        in_maps.append(m)
    res = run_bass_kernel_spmd(nc, in_maps, core_ids=list(range(8)))
    return np.concatenate([r["y"] for r in res.results], axis=0)
